# version 05 (final)
# baseline (speedup 1.0000x reference)
import math
import numpy as np
import ml_dtypes

import concourse.bass as bass
import concourse.mybir as mybir
from concourse.bass_utils import run_bass_kernel_spmd

F32 = mybir.dt.float32
BF16 = mybir.dt.bfloat16
AF = mybir.ActivationFunctionType
ALU = mybir.AluOpType
AX = mybir.AxisListType.X

B, T, E = 4, 2048, 4096
H, HKV, DH = 32, 8, 128
G = H // HKV
HID = 11008
NT = T // 128
NOWN = NT // 2
DCH = E // 128
HCH = HID // 128
EPS = 1e-5
SCALE = 1.0 / math.sqrt(DH)
NEG = -1.0e30

NF4_CODE = np.array([
    -1.0, -0.6961928009986877, -0.5250730514526367, -0.39491748809814453,
    -0.28444138169288635, -0.18477343022823334, -0.09105003625154495, 0.0,
    0.07958029955625534, 0.16093020141124725, 0.24611230194568634,
    0.33791524171829224, 0.44070982933044434, 0.5626170039176941,
    0.7229568362236023, 1.0], dtype=np.float32)
NF4_BOUNDS = ((NF4_CODE[:-1] + NF4_CODE[1:]) * np.float32(0.5)).astype(np.float32)
NF4_DELTA = (NF4_CODE[1:] - NF4_CODE[:-1]).astype(np.float32)
NF4_C0 = float(NF4_CODE[0])
N_DVE_B = 9

W1_WINS = [512] * 21 + [256]
W12_AG_CHUNKS = [(0, 4), (4, 8), (8, 12), (12, 16), (16, 20), (20, 22)]
BOUNDS = [float(x) for x in NF4_BOUNDS]
DELTAS = [float(x) for x in NF4_DELTA]


def _rope_tables():
    ROPE_THETA, ROPE_FACTOR = 500000.0, 32.0
    HI, LO, ORIG = 4.0, 1.0, 8192
    inv = (1.0 / ROPE_THETA ** (np.arange(0, DH, 2, dtype=np.float32) / DH)
           ).astype(np.float32)
    wl = (np.float32(2.0 * math.pi) / inv).astype(np.float32)
    cut_lo, cut_hi = ORIG / HI, ORIG / LO
    inv = np.where(wl > cut_hi, inv / np.float32(ROPE_FACTOR), inv).astype(np.float32)
    smooth = ((np.float32(ORIG) / wl - np.float32(LO)) /
              np.float32(HI - LO)).astype(np.float32)
    smoothed = ((np.float32(1.0) - smooth) * inv / np.float32(ROPE_FACTOR)
                + smooth * inv).astype(np.float32)
    med = (~(wl < cut_hi)) & (~(wl > cut_lo))
    inv = np.where(med, smoothed, inv).astype(np.float32)
    pos = np.arange(T, dtype=np.float32)
    freqs = (pos[:, None] * inv[None, :]).astype(np.float32)
    emb = np.concatenate([freqs, freqs], axis=1)
    return np.cos(emb).astype(np.float32), np.sin(emb).astype(np.float32)


def host_prep(L0_attn_x, L0_attn_Wq, L0_attn_Wk, L0_attn_Wv, L0_attn_Wo,
              L0_ffn_w1, L0_ffn_w2, L0_ffn_w3):
    cosT, sinT = _rope_tables()
    tri = np.where(np.arange(128)[None, :] <= np.arange(128)[:, None],
                   np.float32(0.0), np.float32(NEG)).astype(np.float32)
    id128 = np.eye(128, dtype=np.float32)
    lhsT8 = np.zeros((4, 64, 128), dtype=np.float32)
    for m in range(4):
        for g in range(2):
            for p in range(128):
                lhsT8[m, 32 * g + 2 * m + p // 64, p] = 1.0
    rotP = np.zeros((128, 128), dtype=np.float32)
    for d in range(64):
        rotP[d + 64, d] = -1.0
        rotP[d, d + 64] = 1.0

    xx = np.asarray(L0_attn_x)
    wqT = np.ascontiguousarray(np.asarray(L0_attn_Wq).T)
    wkT = np.ascontiguousarray(np.asarray(L0_attn_Wk).T)
    wvT = np.ascontiguousarray(np.asarray(L0_attn_Wv).T)
    woT = np.ascontiguousarray(np.asarray(L0_attn_Wo).T)
    w1 = np.asarray(L0_ffn_w1)
    w2 = np.asarray(L0_ffn_w2)
    w3 = np.asarray(L0_ffn_w3)

    in_maps, perms = [], []
    for c in range(8):
        b, h = c // 2, c % 2
        own = [2 * i + h for i in range(NOWN)]
        other = [2 * i + (1 - h) for i in range(NOWN)]
        rows = np.concatenate([np.arange(t * 128, (t + 1) * 128)
                               for t in own + other])
        perms.append(rows)
        odiag = np.zeros((NOWN, 128, 128), dtype=np.float32)
        if h == 0:
            odiag[:] = NEG
        d0 = 512 * c
        in_maps.append({
            "x": np.ascontiguousarray(xx[b][rows]),
            "wqT": wqT, "wkT": wkT, "wvT": wvT, "woT": woT,
            "cos": np.ascontiguousarray(cosT[rows].T),
            "sin": np.ascontiguousarray(sinT[rows].T),
            "tri16": tri.astype(ml_dtypes.bfloat16),
            "od16": np.ascontiguousarray(
                odiag.transpose(1, 0, 2).reshape(128, NOWN * 128)
            ).astype(ml_dtypes.bfloat16),
            "idm": id128, "idm16": id128.astype(ml_dtypes.bfloat16),
            "lhsT8": lhsT8.reshape(256, 128), "rotP": rotP,
            "w1T_s": np.ascontiguousarray(w1[:, d0:d0 + 512].T),
            "w1rm_s": np.ascontiguousarray(w1[:, d0:d0 + 512]),
            "w2T_s": np.ascontiguousarray(w2[:, d0:d0 + 512].T),
            "w2rm_s": np.ascontiguousarray(w2[:, d0:d0 + 512]),
            "w3T_s": np.ascontiguousarray(w3[d0:d0 + 512, :].T),
            "w3rm_s": np.ascontiguousarray(w3[d0:d0 + 512, :]),
        })
    return in_maps, perms


class Sem:
    def __init__(self, nc, name):
        self.h = nc.semaphore(name).__enter__()
        self.count = 0


class Prog:
    ENGINES = ("sync", "scalar", "tensor", "vector", "gpsimd")

    def __init__(self, nc):
        self.nc = nc
        self.streams = {e: [] for e in self.ENGINES}
        self.esem = {e: Sem(nc, f"edge_{e}") for e in ("scalar", "vector",
                                                       "gpsimd")}
        self.pending = {e: None for e in self.ENGINES}

    def sem(self, name):
        return Sem(self.nc, name)

    def emit(self, eng, fn, waits=(), incs=()):
        waits = list(waits)
        if self.pending.get(eng) is not None:
            waits.append(self.pending[eng])
            self.pending[eng] = None
        for s, k in incs:
            s.count += k
        self.streams[eng].append([fn, waits, list(incs)])

    def edge(self, eng):
        last = self.streams[eng][-1]
        if last[2]:
            sem, _ = last[2][0]
            self.pending[eng] = (sem, sem.count)
        else:
            es = self.esem[eng]
            es.count += 1
            last[2].append((es, 1))
            self.pending[eng] = (es, es.count)

    def install(self, block):
        def mk(name):
            def body(e):
                for fn, waits, incs in self.streams[name]:
                    for s, v in waits:
                        e.wait_ge(s.h, v)
                    ins = fn(e)
                    for s, k in incs:
                        ins.then_inc(s.h, k)
            return body
        block.sync(mk("sync"))
        block.scalar(mk("scalar"))
        block.tensor(mk("tensor"))
        block.vector(mk("vector"))
        block.gpsimd(mk("gpsimd"))


def ts(i, n=128):
    return slice(i * n, (i + 1) * n)


class Builder:
    def __init__(self, stage="full"):
        self.stage = stage
        self.nc = bass.Bass("TRN2", target_bir_lowering=False)
        self.p = Prog(self.nc)
        self._cms = {}

    def sb(self, name, shape, dt):
        cm = self.nc.sbuf_tensor(name, shape, dt)
        t = cm.__enter__()
        self._cms[name] = cm
        return t

    def psm(self, name, shape, dt):
        cm = self.nc.psum_tensor(name, shape, dt)
        t = cm.__enter__()
        self._cms[name] = cm
        return t

    def free(self, *names):
        order = list(self._cms.keys())
        for n in sorted(names, key=order.index, reverse=True):
            self._cms.pop(n).__exit__(None, None, None)

    def build(self):
        nc, p = self.nc, self.p
        st = self.stage

        def din(n, s):
            return nc.dram_tensor(n, s, F32, kind="ExternalInput").ap()

        self.x = din("x", [T, E])
        self.wqT, self.wkT = din("wqT", [E, E]), din("wkT", [E, HKV * DH])
        self.wvT, self.woT = din("wvT", [E, HKV * DH]), din("woT", [E, E])
        self.cos_d, self.sin_d = din("cos", [DH, T]), din("sin", [DH, T])
        self.tri_d = nc.dram_tensor("tri16", [128, 128], BF16,
                                    kind="ExternalInput").ap()
        self.odiag_d = nc.dram_tensor("od16", [128, NOWN * 128], BF16,
                                      kind="ExternalInput").ap()
        self.idm_d, self.lhsT8_d = din("idm", [128, 128]), din("lhsT8", [256, 128])
        self.idm16_d = nc.dram_tensor("idm16", [128, 128], BF16,
                                      kind="ExternalInput").ap()
        self.rotP_d = din("rotP", [128, 128])
        self.w_in = {
            ("w1", "T"): din("w1T_s", [512, HID]),
            ("w1", "rm"): din("w1rm_s", [HID, 512]),
            ("w2", "T"): din("w2T_s", [512, HID]),
            ("w2", "rm"): din("w2rm_s", [HID, 512]),
            ("w3", "T"): din("w3T_s", [HID, 512]),
            ("w3", "rm"): din("w3rm_s", [512, HID]),
        }
        self.out = nc.dram_tensor("out", [NOWN * 128, E], F32,
                                  kind="ExternalOutput").ap()
        dr = lambda n, s, t: nc.dram_tensor(n, s, t).ap()
        self.q_dram = dr("q_dram", [H * DH, NOWN * 128], BF16)
        self.k_dram = dr("k_dram", [HKV * DH, T], BF16)
        self.v_dram = dr("v_dram", [T, HKV * DH], BF16)
        self.ctx_dram = dr("ctx_dram", [E, NOWN * 128], BF16)
        self.s_dram = dr("s_dram", [E, NOWN * 128], F32)
        self.xn_dram = dr("xn_dram", [E, NOWN * 128], BF16)
        self.dq_sh, self.dq_full = {}, {}
        for w in ("w1", "w2"):
            for k, (a, bb) in enumerate(W12_AG_CHUNKS):
                wd = sum(W1_WINS[a:bb])
                self.dq_sh[(w, k)] = dr(f"{w}_sh{k}", [512, wd], BF16)
                self.dq_full[(w, k)] = nc.dram_tensor(
                    f"{w}_fl{k}", [E, wd], BF16, addr_space="Shared").ap()
        self.dq_sh[("w3", 0)] = dr("w3_sh0", [HID, 512], BF16)
        self.dq_full[("w3", 0)] = nc.dram_tensor(
            "w3_fl0", [8 * HID, 512], BF16, addr_space="Shared").ap()

        if st in ("proj", "attn"):
            self.dbg_q = nc.dram_tensor("dbg_q", [H * DH, NOWN * 128], BF16,
                                        kind="ExternalOutput").ap()
            self.dbg_k = nc.dram_tensor("dbg_k", [HKV * DH, T], BF16,
                                        kind="ExternalOutput").ap()
            self.dbg_v = nc.dram_tensor("dbg_v", [T, HKV * DH], BF16,
                                        kind="ExternalOutput").ap()
            self.dbg_xn = nc.dram_tensor("dbg_xn", [E, NOWN * 128], BF16,
                                         kind="ExternalOutput").ap()
        if st == "attn":
            self.dbg_s = nc.dram_tensor("dbg_s", [E, NOWN * 128], F32,
                                        kind="ExternalOutput").ap()
            self.dbg_ctx = nc.dram_tensor("dbg_ctx", [E, NOWN * 128], BF16,
                                          kind="ExternalOutput").ap()
        if st == "deq":
            self.dbg_dq = {}
            for key, t in self.dq_full.items():
                w, k = key
                self.dbg_dq[key] = nc.dram_tensor(
                    f"dbg_{w}_{k}", list(t.tensor.shape), BF16,
                    kind="ExternalOutput").ap()

        self.s_ld = p.sem("s_ld")
        self.s_ld2 = p.sem("s_ld2")
        self.s_cast = p.sem("s_cast")
        self.s_pe = p.sem("s_pe")
        self.s_dv = p.sem("s_dv")
        self.s_ac = p.sem("s_ac")
        self.s_gp = p.sem("s_gp")
        self.s_out = p.sem("s_out")
        self.s_cc = p.sem("s_cc")
        self.s_dqi = p.sem("s_dqi")
        self.s_dqo = p.sem("s_dqo")

        self.c_tri = self.sb("c_tri", [128, 128], BF16)
        self.c_od = self.sb("c_od", [128, NOWN * 128], BF16)
        self.c_id32 = self.sb("c_id32", [128, 128], F32)
        self.c_id16 = self.sb("c_id16", [128, 128], BF16)
        self.c_rot = self.sb("c_rot", [128, 128], F32)
        self.c_m8 = [self.sb(f"c_m8_{m}", [64, 128], F32) for m in range(4)]
        self.stat = self.sb("stat", [128, 8], F32)
        self.absT = self.sb("absT", [64, 11 * 512], F32)
        W = 2048
        self.dq_in = [self.sb(f"dq_in{i}", [128, W], F32) for i in range(2)]
        self.dq_red = self.sb("dq_red", [128, 8], F32)
        self.dq_r = [self.sb("dq_r0", [128, W], F32)]
        self.dq_as = [self.sb("dq_as0", [128, W], F32)]
        self.dq_B = [self.sb(f"dq_B{i}", [128, W], F32) for i in range(2)]
        self.dq_A = [self.sb("dq_A0", [128, W], F32)]
        self.dq_o = [self.sb("dq_o0", [128, W], BF16)]
        self.stage8 = [self.sb(f"stage8_{i}", [8, 128], F32) for i in range(2)]

        self.ps_bcR = self.psm("ps_bcR", [128, 512], F32)
        self.ps_bcA = self.psm("ps_bcA", [128, 512], F32)
        self.ps_p = [self.psm(f"ps_p{i}", [128, 512], F32) for i in range(2)]
        self.ps_sc = self.psm("ps_sc", [128, 2048], F32)

        self.ag_done = {}
        self.load_constants()
        self.deq = self.dequant_gen()
        self.pull_deq(1)
        self.ln_proj()
        if st == "proj":
            self.finish()
            return nc
        if st == "deq":
            self.pull_deq(10 ** 9)
            self.finish()
            return nc
        self.attention()
        self.wo_phase()
        if st == "attn":
            self.finish()
            return nc
        self.pull_deq(10 ** 9)
        self.ffn()
        self.finish()
        return nc

    def load_constants(self):
        p, ld = self.p, self.s_ld
        D = lambda o, i: p.emit("sync", lambda e, o=o, i=i:
                                e.dma_start(out=o, in_=i), incs=[(self.s_ld, 16)])
        D(self.c_tri[:, :], self.tri_d)
        D(self.c_id32[:, :], self.idm_d)
        D(self.c_id16[:, :], self.idm16_d)
        D(self.c_rot[:, :], self.rotP_d)
        D(self.c_od[:, :], self.odiag_d)
        for m in range(4):
            D(self.c_m8[m][:, :], self.lhsT8_d[ts(m, 64), :])
        self.n_const = self.s_ld.count

    def dequant_gen(self):
        p, s = self.p, self
        ld, pe, dv, ac, gp = self.s_dqi, self.s_pe, self.s_dv, self.s_ac, self.s_gp
        dqo = self.s_dqo
        in_free = [None, None]
        ra_free = [None, None]
        o_free = [None]
        ki = 0
        km = 0

        last_main_pe = None
        for w in ("w1", "w2", "w3"):
            rms = []
            if w in ("w1", "w2"):
                for t in range(HCH):
                    wi = t // 4
                    rms.append((self.w_in[(w, "rm")][ts(t), :], 512,
                                32 * (wi % 2), (wi // 2) * 512 + (t % 4) * 128, 128))
            else:
                for ct in range(4):
                    col = 0
                    for hw, win in enumerate(W1_WINS):
                        rms.append((self.w_in[("w3", "rm")][ts(ct), col:col + win],
                                    win, 32 * (hw % 2),
                                    (hw // 2) * 512 + ct * 128, 128))
                        col += win
            ab_ev = None
            st_free = [None, None]
            nrm = 0
            for (src, win, prow, dcol, dwid) in rms:
                buf = s.dq_in[ki % 2]
                p.emit("sync", lambda e, o=buf, i=src, win=win:
                       e.dma_start(out=o[:, 0:win], in_=i),
                       waits=([in_free[ki % 2]] if in_free[ki % 2] else []),
                       incs=[(ld, 16)])
                w_ld = ld.count
                nb = win // 64
                prev_tr = getattr(self, "_abs_tr", None)
                p.emit("vector", lambda e, b=buf, nb=nb:
                       e.reduce_max(s.dq_red[:, 0:nb],
                                    b[:, 0:nb * 64].rearrange(
                                        "p (b e) -> p b e", e=64),
                                    axis=AX, apply_absolute_value=True),
                       waits=[(ld, w_ld)] + ([prev_tr] if prev_tr else []),
                       incs=[(dv, 1)])
                w_red = dv.count
                in_free[ki % 2] = (dv, w_red)
                waits = [(dv, w_red), (self.s_ld, self.n_const)]
                if ab_ev:
                    waits.append(ab_ev)
                if last_main_pe:
                    waits.append(last_main_pe)
                    waits.append((ac, ac.count))
                p.emit("tensor", lambda e, nb=nb:
                       e.transpose(s.ps_bcR[0:nb, 0:128],
                                   s.dq_red[:, 0:nb], s.c_id32[:, :]),
                       waits=waits)
                p.emit("tensor", lambda e:
                       e.transpose(s.ps_bcR[0:8, 128:256],
                                   s.dq_red[:, 0:8], s.c_id32[:, :]),
                       incs=[(pe, 1)])
                w_tr = pe.count
                self._abs_tr = (pe, w_tr)
                stg = s.stage8[nrm % 2]
                p.emit("vector", lambda e, stg=stg, dwid=dwid, nb=nb:
                       e.tensor_copy(stg[0:nb, 0:dwid], s.ps_bcR[0:nb, 0:dwid]),
                       waits=[(pe, w_tr)] +
                             ([st_free[nrm % 2]] if st_free[nrm % 2] else []),
                       incs=[(dv, 1)])
                ab_ev = (dv, dv.count)
                p.emit("sync", lambda e, stg=stg, prow=prow, dcol=dcol,
                       dwid=dwid, nb=nb:
                       e.dma_start(out=s.absT[prow:prow + nb, dcol:dcol + dwid],
                                   in_=stg[0:nb, 0:dwid]),
                       waits=[ab_ev], incs=[(self.s_ld2, 16)])
                st_free[nrm % 2] = (self.s_ld2, self.s_ld2.count)
                nrm += 1
                ki += 1
                yield
            abs_filled = (self.s_ld2, self.s_ld2.count)
            rec_done = abs_filled
            yield

            mains = []
            if w in ("w1", "w2"):
                for ci, (a0, b0) in enumerate(W12_AG_CHUNKS):
                    ccol = 0
                    for wi in range(a0, b0):
                        win = W1_WINS[wi]
                        col = sum(W1_WINS[:wi])
                        for a in range(4):
                            mains.append(
                                (self.w_in[(w, "T")][ts(a), col:col + win],
                                 a, wi, win,
                                 self.dq_sh[(w, ci)][ts(a), ccol:ccol + win],
                                 (w, ci) if wi == b0 - 1 and a == 3 else None))
                        ccol += win
            else:
                for t in range(HCH):
                    mains.append((self.w_in[("w3", "T")][ts(t), :],
                                  t % 4, t // 4, 512,
                                  self.dq_sh[("w3", 0)][ts(t), :],
                                  ("w3", 0) if t == HCH - 1 else None))

            bc_ev = None
            bi = 0
            while mains:
                batch = mains[:4]
                mains = mains[4:]
                offs, o = [], 0
                for (_, _, _, win, _, _) in batch:
                    offs.append(o)
                    o += win
                bw = o
                buf = s.dq_in[bi % 2]
                for k4, (srcT, m, wi, win, dst, agk) in enumerate(batch):
                    p.emit("sync", lambda e, o=buf, i=srcT, off=offs[k4],
                           win=win: e.dma_start(out=o[:, off:off + win], in_=i),
                           waits=(([in_free[bi % 2]] if in_free[bi % 2] else [])
                                  if k4 == 0 else []),
                           incs=[(ld, 16)])
                w_ld = ld.count
                rbuf, abuf = s.dq_r[0], s.dq_as[0]
                for k4, (srcT, m, wi, win, dst, agk) in enumerate(batch):
                    prow = 32 * (wi % 2)
                    ccol2 = (wi // 2) * 512
                    asl = s.absT[prow:prow + 8, ccol2:ccol2 + win]
                    bw_w = [rec_done]
                    if bc_ev:
                        bw_w.append(bc_ev)
                    p.emit("tensor", lambda e, m=m, asl=asl, win=win, prow=prow:
                           e.matmul(s.ps_bcA[:, 0:win],
                                    s.c_m8[m][prow:prow + 8, :], asl,
                                    start=True, stop=True),
                           waits=bw_w)
                    p.emit("tensor", lambda e:
                           e.transpose(s.ps_bcR[0:8, 128:256],
                                       s.dq_red[:, 0:8], s.c_id32[:, :]),
                           incs=[(pe, 1)])
                    w_bc = pe.count
                    last_main_pe = (pe, w_bc)
                    p.emit("scalar", lambda e, off=offs[k4], win=win, abuf=abuf:
                           e.copy(abuf[:, off:off + win], s.ps_bcA[:, 0:win]),
                           waits=[(pe, w_bc)] +
                                 ([ra_free[0]] if k4 == 0 and ra_free[0]
                                  else []),
                           incs=[(ac, 1)])
                    bc_ev = (ac, ac.count)
                w_ra = ac.count
                nbuf = buf
                p.emit("vector", lambda e, rbuf=rbuf, abuf=abuf, bw=bw:
                       e.reciprocal(rbuf[:, 0:bw], abuf[:, 0:bw]),
                       waits=[(ac, w_ra)], incs=[(dv, 1)])
                p.edge("vector")
                p.emit("vector", lambda e, b=buf, rbuf=rbuf, bw=bw:
                       e.tensor_tensor(b[:, 0:bw], b[:, 0:bw],
                                       rbuf[:, 0:bw], ALU.mult),
                       waits=[(ld, w_ld)],
                       incs=[(dv, 1)])
                p.edge("vector")
                B, A = s.dq_B, s.dq_A

                def TSW(dst, j, nbuf=nbuf, bw=bw):
                    p.emit("vector", lambda e, dst=dst, j=j, nbuf=nbuf, bw=bw:
                           e.tensor_scalar(dst[:, 0:bw], nbuf[:, 0:bw],
                                           BOUNDS[j], DELTAS[j],
                                           ALU.is_gt, ALU.mult))

                def TTW(dst, a0, b0, bw=bw):
                    p.emit("vector", lambda e, dst=dst, a0=a0, b0=b0, bw=bw:
                           e.tensor_tensor(dst[:, 0:bw], a0[:, 0:bw],
                                           b0[:, 0:bw], ALU.add))
                TSW(A[0], 0); TSW(B[0], 1); TSW(B[1], 2)
                p.edge("vector")
                for r in range(6):
                    TTW(A[0], A[0], B[0]); TSW(B[0], 3 + 2 * r)
                    p.edge("vector")
                    TTW(A[0], A[0], B[1]); TSW(B[1], 4 + 2 * r)
                    p.edge("vector")
                TTW(A[0], A[0], B[0])
                p.edge("vector")
                TTW(A[0], A[0], B[1])
                p.edge("vector")
                obuf = s.dq_o[0]
                p.emit("vector", lambda e, abuf=abuf, obuf=obuf, bw=bw:
                       e.scalar_tensor_tensor(obuf[:, 0:bw], A[0][:, 0:bw],
                                              NF4_C0, abuf[:, 0:bw],
                                              ALU.add, ALU.mult),
                       waits=([o_free[0]] if o_free[0] else []),
                       incs=[(dv, 1)])
                w_dq = dv.count
                in_free[bi % 2] = (dv, w_dq)
                ra_free[0] = (dv, w_dq)
                for k4, (srcT, m, wi, win, dst, agk) in enumerate(batch):
                    p.emit("sync", lambda e, obuf=obuf, dst=dst, off=offs[k4],
                           win=win: e.dma_start(out=dst,
                                                in_=obuf[:, off:off + win]),
                           waits=([(dv, w_dq)] if k4 == 0 else []),
                           incs=[(dqo, 16)])
                    if agk is not None:
                        w_st = dqo.count
                        p.emit("gpsimd", lambda e, key=agk:
                               e.collective_compute(
                                   "AllGather", ALU.bypass,
                                   replica_groups=[list(range(8))],
                                   ins=[self.dq_sh[key].opt()],
                                   outs=[self.dq_full[key].opt()]),
                               waits=[(dqo, w_st)], incs=[(self.s_cc, 1)])
                        self.ag_done[agk] = self.s_cc.count
                o_free[0] = (dqo, dqo.count)
                bi += 1
                yield

        if self.stage == "deq":
            for key, dstap in self.dbg_dq.items():
                p.emit("sync", lambda e, key=key, dstap=dstap:
                       e.dma_start(out=dstap, in_=self.dq_full[key].opt()),
                       waits=[(self.s_cc, self.ag_done[key])],
                       incs=[(self.s_out, 16)])
        self._deq_done = True
        yield

    def pull_deq(self, n):
        for _ in range(n):
            if getattr(self, "_deq_done", False):
                break
            try:
                next(self.deq)
            except StopIteration:
                self._deq_done = True
                break

    def ln_proj(self):
        p, s = self.p, self
        ld, ld2, cast = self.s_ld, self.s_ld2, self.s_cast
        pe, dv, ac = self.s_pe, self.s_dv, self.s_ac

        x_t = self.sb("x_t", [128, E], F32)
        sq_t = self.sb("sq_t", [128, E], F32)
        rp_t = self.sb("rp_t", [128, 1024], F32)
        kf32 = self.sb("kf32", [128, 512], F32)
        xn_t = self.sb("xn_t", [128, E], F32)
        xn_fm = self.sb("xn_fm", [128, DCH * 512], BF16)
        wpan = [self.sb(f"wpan{i}", [128, DCH * 128], BF16) for i in range(2)]
        wvsl = [self.sb(f"wvsl{i}", [128, 512], BF16) for i in range(2)]
        kq_sb = [self.sb(f"kq_sb{i}", [128, 512], BF16) for i in range(2)]
        v_sb = [self.sb(f"v_sb{i}", [128, 512], BF16) for i in range(2)]
        c_cos = self.sb("c_cos", [128, 512], F32)
        c_sin = self.sb("c_sin", [128, 512], F32)

        pan_free = [None, None]
        kq_free = [None, None]
        vsb_free = [None, None]
        wv_free = [None, None]
        tr_ev = [None, None]
        self._rope_dv = 0
        self._kf_free = 0

        def load_panel(dst, wT, fc, idx):
            src = wT[:, ts(fc)].rearrange("(c p) f -> p c f", p=128)
            p.emit("gpsimd", lambda e, o=dst, i=src: e.dma_start(
                out=o.ap().rearrange("p (c f) -> p c f", f=128), in_=i),
                   waits=([pan_free[idx % 2]] if pan_free[idx % 2] else []),
                   incs=[(cast, 16)])
            return cast.count

        def rope_store(psum, idx, dest, w_acc):
            kq = kq_sb[idx % 2]
            p.emit("tensor", lambda e: e.transpose(
                self.ps_sc[:, 0:128], rp_t[:, 0:128], self.c_id32[:, :]),
                waits=[(dv, dv.count)], incs=[(pe, 1)])
            w_acc2 = pe.count
            p.emit("scalar", lambda e, ps=psum: e.copy(kf32[:, :], ps[:, :]),
                   waits=[(pe, w_acc2), (dv, self._kf_free)], incs=[(ac, 1)])
            w_ev = ac.count
            p.emit("tensor", lambda e, ps=psum: e.matmul(
                ps[:, :], self.c_rot[:, :], kf32[:, :],
                start=True, stop=True),
                waits=[(ac, w_ev)], incs=[(pe, 1)])
            w_rot = pe.count
            wl = [(pe, w_rot)] + ([kq_free[idx % 2]] if kq_free[idx % 2] else [])
            p.emit("vector", lambda e: e.tensor_tensor(
                rp_t[:, 0:512], kf32[:, :], c_cos[:, :], ALU.mult), waits=wl)
            p.emit("vector", lambda e, ps=psum: e.tensor_tensor(
                rp_t[:, 512:1024], ps[:, :], c_sin[:, :], ALU.mult))
            p.edge("vector")
            p.emit("vector", lambda e, kq=kq: e.tensor_tensor(
                kq[:, :], rp_t[:, 0:512], rp_t[:, 512:1024], ALU.add),
                incs=[(dv, 1)])
            w_rope = dv.count
            self._kf_free = w_rope
            p.emit("scalar", lambda e, kq=kq, d=dest:
                   e.dma_start(out=d, in_=kq[:, :]),
                   waits=[(dv, w_rope)], incs=[(ld2, 16)])
            kq_free[idx % 2] = (ld2, ld2.count)
            return w_rope

        nK = 0
        for g4 in range(4):
            own = g4 < 2
            p.emit("sync", lambda e, g4=g4: e.dma_start(
                out=c_cos[:, :], in_=self.cos_d[:, ts(g4, 512)]),
                waits=[(dv, dv.count)], incs=[(ld, 16)])
            p.emit("sync", lambda e, g4=g4: e.dma_start(
                out=c_sin[:, :], in_=self.sin_d[:, ts(g4, 512)]),
                incs=[(ld, 16)])

            for tl in range(4):
                tile = g4 * 4 + tl
                p.emit("sync", lambda e, tile=tile: e.dma_start(
                    out=x_t[:, :], in_=self.x[ts(tile), :]),
                    waits=[(dv, dv.count), (ac, ac.count)], incs=[(ld, 16)])
                w_x = ld.count
                p.emit("scalar", lambda e: e.activation(
                    sq_t[:, :], x_t[:, :], AF.Square, accum_out=s.stat[:, 1:2]),
                    waits=[(ld, w_x)], incs=[(ac, 1)])
                w_sq = ac.count
                p.emit("vector", lambda e: e.reduce_sum(
                    s.stat[:, 0:1], x_t[:, :], axis=AX), waits=[(ld, w_x)])
                p.edge("vector")
                p.emit("vector", lambda e: e.tensor_scalar(
                    s.stat[:, 2:3], s.stat[:, 0:1], 1.0 / E, None, ALU.mult))
                p.edge("vector")
                p.emit("vector", lambda e: e.tensor_tensor(
                    s.stat[:, 3:4], s.stat[:, 2:3], s.stat[:, 2:3], ALU.mult))
                p.emit("vector", lambda e: e.tensor_scalar(
                    s.stat[:, 4:5], s.stat[:, 1:2], 1.0 / E, float(EPS),
                    ALU.mult, ALU.add), waits=[(ac, w_sq)])
                p.edge("vector")
                p.emit("vector", lambda e: e.tensor_tensor(
                    s.stat[:, 4:5], s.stat[:, 4:5], s.stat[:, 3:4],
                    ALU.subtract), incs=[(dv, 1)])
                w_var = dv.count
                p.emit("scalar", lambda e: e.activation(
                    s.stat[:, 5:6], s.stat[:, 4:5], AF.Ln),
                    waits=[(dv, w_var)])
                p.edge("scalar")
                p.emit("scalar", lambda e: e.activation(
                    s.stat[:, 6:7], s.stat[:, 5:6], AF.Exp, scale=-0.5),
                    incs=[(ac, 1)])
                w_rstd = ac.count
                p.emit("vector", lambda e: e.tensor_scalar(
                    xn_t[:, :], x_t[:, :], s.stat[:, 2:3], s.stat[:, 6:7],
                    ALU.subtract, ALU.mult),
                    waits=[(ac, w_rstd), (pe, pe.count)], incs=[(dv, 1)])
                w_xn = dv.count
                grp_bar = []
                if tl == 0:
                    grp_bar = [(pe, pe.count), (ld2, ld2.count)]
                w_trs = [None]
                for dc in range(DCH):
                    bb = dc % 2
                    wl = [(dv, w_xn), (self.s_ld, self.n_const)]
                    if tr_ev[bb]:
                        wl.append(tr_ev[bb])
                    p.emit("tensor", lambda e, dc=dc, bb=bb: e.transpose(
                        self.ps_p[bb][:, 0:128], xn_t[:, ts(dc)],
                        self.c_id32[:, :]), waits=wl, incs=[(pe, 1)])
                    w_trs.append(pe.count)
                    if dc >= 1:
                        dp = dc - 1
                        p.emit("vector", lambda e, dp=dp, tl=tl:
                               e.tensor_copy(xn_fm[:, dp * 512 + tl * 128:
                                                    dp * 512 + tl * 128 + 128],
                                             self.ps_p[dp % 2][:, 0:128]),
                               waits=[(pe, w_trs[dc + 1])] +
                                     (grp_bar if dp == 0 else []),
                               incs=[(dv, 1)])
                        tr_ev[dp % 2] = (dv, dv.count)
                wl = [tr_ev[0]] if tr_ev[0] else []
                p.emit("tensor", lambda e: e.transpose(
                    self.ps_p[0][:, 0:128], xn_t[:, 0:128],
                    self.c_id32[:, :]), waits=wl, incs=[(pe, 1)])
                w_dmy = pe.count
                dp = DCH - 1
                p.emit("vector", lambda e, dp=dp, tl=tl:
                       e.tensor_copy(xn_fm[:, dp * 512 + tl * 128:
                                            dp * 512 + tl * 128 + 128],
                                     self.ps_p[dp % 2][:, 0:128]),
                       waits=[(pe, w_dmy)], incs=[(dv, 1)])
                tr_ev[dp % 2] = (dv, dv.count)
                tr_ev[0] = (dv, dv.count)
                self.pull_deq(2)
            w_xn_all = (self.s_dv, dv.count)

            if own:
                xsrc = xn_fm.ap().rearrange("p (c t) -> p c t", t=512)
                xdst = self.xn_dram[:, ts(g4, 512)].rearrange(
                    "(c p) t -> p c t", p=128)
                p.emit("scalar", lambda e, xsrc=xsrc, xdst=xdst:
                       e.dma_start(out=xdst, in_=xsrc),
                       waits=[w_xn_all], incs=[(ld2, 16)])

            for fc in range(HKV):
                w_pan = load_panel(wpan[fc % 2], self.wkT, fc, fc)
                psum = self.ps_p[fc % 2]
                for dc in range(DCH):
                    wl = []
                    if dc == 0:
                        wl = [(cast, w_pan), w_xn_all]
                        if tr_ev[fc % 2]:
                            wl.append(tr_ev[fc % 2])
                        if self._rope_dv:
                            wl.append((dv, self._rope_dv))
                    p.emit("tensor", lambda e, fc=fc, dc=dc, psum=psum:
                           e.matmul(psum[:, :], wpan[fc % 2][:, ts(dc)],
                                    xn_fm[:, ts(dc, 512)],
                                    start=(dc == 0), stop=(dc == DCH - 1)),
                           waits=wl, incs=([(pe, 1)] if dc == DCH - 1 else []))
                w_acc = pe.count
                pan_free[fc % 2] = (pe, w_acc)
                tr_ev[fc % 2] = None
                self._rope_dv = rope_store(
                    psum, nK, self.k_dram[ts(fc), ts(g4, 512)], w_acc)
                nK += 1
                self.pull_deq(1)

            for fh in range(2):
                for dc in range(DCH):
                    src = self.wvT[ts(dc), ts(fh, 512)]
                    p.emit("gpsimd", lambda e, o=wvsl[dc % 2], i=src:
                           e.dma_start(out=o[:, :], in_=i),
                           waits=([wv_free[dc % 2]] if wv_free[dc % 2] else []),
                           incs=[(cast, 16)])
                    w_wv = cast.count
                    for tl in range(4):
                        wl = []
                        if dc == 0 and tl == 0:
                            wl = [w_xn_all]
                            if getattr(self, "_v_ev", None):
                                wl.append(self._v_ev)
                        if tl == 0:
                            wl.append((cast, w_wv))
                        p.emit("tensor", lambda e, dc=dc, tl=tl, fh=fh:
                               e.matmul(self.ps_sc[:, ts(tl, 512)],
                                        xn_fm[:, dc * 512 + tl * 128:
                                              dc * 512 + (tl + 1) * 128],
                                        wvsl[dc % 2][:, :],
                                        start=(dc == 0), stop=(dc == DCH - 1)),
                               waits=wl,
                               incs=([(pe, 1)] if tl == 3 else []))
                    wv_free[dc % 2] = (pe, pe.count)
                p.emit("tensor", lambda e: e.transpose(
                    self.ps_p[0][:, 0:128], rp_t[:, 0:128], self.c_id32[:, :]),
                    waits=[(dv, dv.count)], incs=[(pe, 1)])
                w_vacc = pe.count
                for tl in range(4):
                    vb = v_sb[tl % 2]
                    p.emit("vector", lambda e, tl=tl, vb=vb:
                           e.tensor_copy(vb[:, :], self.ps_sc[:, ts(tl, 512)]),
                           waits=[(pe, w_vacc)] +
                                 ([vsb_free[tl % 2]] if vsb_free[tl % 2] else []),
                           incs=[(dv, 1)])
                    w_ev = dv.count
                    if tl == 3:
                        self._v_ev = (dv, w_ev)
                    dstv = self.v_dram[ts(g4 * 4 + tl), ts(fh, 512)]
                    p.emit("scalar", lambda e, vb=vb, dstv=dstv:
                           e.dma_start(out=dstv, in_=vb[:, :]),
                           waits=[(dv, w_ev)], incs=[(ld2, 16)])
                    vsb_free[tl % 2] = (ld2, ld2.count)
                self.pull_deq(1)

            if own:
                for fc in range(H):
                    w_pan = load_panel(wpan[fc % 2], self.wqT, fc, fc)
                    psum = self.ps_p[fc % 2]
                    for dc in range(DCH):
                        wl = []
                        if dc == 0:
                            wl = [(cast, w_pan), (dv, self._rope_dv)]
                        p.emit("tensor", lambda e, fc=fc, dc=dc, psum=psum:
                               e.matmul(psum[:, :], wpan[fc % 2][:, ts(dc)],
                                        xn_fm[:, ts(dc, 512)],
                                        start=(dc == 0), stop=(dc == DCH - 1)),
                               waits=wl,
                               incs=([(pe, 1)] if dc == DCH - 1 else []))
                    w_acc = pe.count
                    pan_free[fc % 2] = (pe, w_acc)
                    self._rope_dv = rope_store(
                        psum, fc, self.q_dram[ts(fc), ts(g4, 512)], w_acc)
                    if fc % 4 == 3:
                        self.pull_deq(1)

        if self.stage in ("proj", "attn"):
            w_all = [(ld2, ld2.count), (self.s_dv, dv.count)]
            for (src, dst) in ((self.q_dram, self.dbg_q),
                               (self.k_dram, self.dbg_k),
                               (self.v_dram, self.dbg_v),
                               (self.xn_dram, self.dbg_xn)):
                p.emit("sync", lambda e, src=src, dst=dst:
                       e.dma_start(out=dst, in_=src),
                       waits=w_all, incs=[(self.s_out, 16)])

        self.free("x_t", "sq_t", "rp_t", "kf32", "xn_t", "xn_fm", "wpan0",
                  "wpan1", "wvsl0", "wvsl1", "kq_sb0", "kq_sb1", "v_sb0",
                  "v_sb1", "c_cos", "c_sin")
        self._proj_done = (self.s_ld2, ld2.count)

    def attention(self):
        p, s = self.p, self
        ld, ld2 = self.s_ld, self.s_ld2
        pe, dv, ac = self.s_pe, self.s_dv, self.s_ac

        q_cur = self.sb("q_cur", [128, G * NOWN * 128], BF16)
        k_sp = self.sb("k_sp", [128, 2048], BF16)
        v_sp = self.sb("v_sp", [128, 2048], BF16)
        probs = self.sb("probs", [128, 2048], F32)
        probsT = self.sb("probsT", [128, 2048], BF16)
        ctx_sb = self.sb("ctx_sb", [128, G * NOWN * 128], BF16)

        q_free = None
        kv_free = None
        ctx_st_done = None
        tr_ev = [None, None]
        ctx_ev = None
        exp_done = None

        for kvh in range(HKV):
            qsrc = self.q_dram[ts(kvh, G * DH), :].rearrange(
                "(g p) t -> p g t", p=128)
            wl = [self._proj_done] + ([q_free] if q_free else [])
            p.emit("sync", lambda e, qsrc=qsrc: e.dma_start(
                out=q_cur.ap().rearrange("p (g t) -> p g t", g=G),
                in_=qsrc), waits=wl, incs=[(ld, 16)])
            w_q = ld.count
            for i in range(NOWN):
                span = (i + 1) * 128
                wl = [self._proj_done] + ([kv_free] if kv_free else [])
                p.emit("sync", lambda e, kvh=kvh, span=span: e.dma_start(
                    out=k_sp[:, 0:span], in_=self.k_dram[ts(kvh), 0:span]),
                    waits=wl, incs=[(ld, 16)])
                p.emit("sync", lambda e, kvh=kvh, span=span: e.dma_start(
                    out=k_sp[:, span:2 * span],
                    in_=self.k_dram[ts(kvh), 1024:1024 + span]),
                    incs=[(ld, 16)])
                vsrc_own = self.v_dram[0:span, ts(kvh)].rearrange(
                    "(j p) f -> p j f", p=128)
                vsrc_oth = self.v_dram[1024:1024 + span, ts(kvh)].rearrange(
                    "(j p) f -> p j f", p=128)
                p.emit("sync", lambda e, v=vsrc_own, span=span: e.dma_start(
                    out=v_sp[:, 0:span].rearrange("p (j f) -> p j f", f=128),
                    in_=v), incs=[(ld, 16)])
                p.emit("sync", lambda e, v=vsrc_oth, span=span: e.dma_start(
                    out=v_sp[:, span:2 * span].rearrange(
                        "p (j f) -> p j f", f=128),
                    in_=v), incs=[(ld, 16)])
                w_kv = ld.count
                for g in range(G):
                    qsl = q_cur[:, g * 1024 + i * 128: g * 1024 + (i + 1) * 128]
                    first_wl = [(ld, w_q), (ld, w_kv)]
                    if ctx_ev:
                        first_wl.append(ctx_ev)
                    if exp_done:
                        first_wl.append(exp_done)
                    if kvh == 0 and i == 0 and g == 0:
                        first_wl.append(self._v_ev)
                    nmm = (span + 511) // 512
                    for half, base in ((0, 0), (1, 1024)):
                        for mm in range(nmm):
                            c0 = mm * 512
                            cw = min(512, span - c0)
                            p.emit("tensor",
                                   lambda e, qsl=qsl, half=half, base=base,
                                   c0=c0, cw=cw, span=span:
                                   e.matmul(self.ps_sc[:, base + c0:base + c0 + cw],
                                            qsl, k_sp[:, half * span + c0:
                                                      half * span + c0 + cw],
                                            start=True, stop=True),
                                   waits=(first_wl if half == 0 and mm == 0
                                          else []))
                    p.emit("tensor", lambda e, i=i: e.matmul(
                        self.ps_sc[:, ts(i)], self.c_id16[:, :],
                        self.c_tri[:, :], start=False, stop=True,
                        skip_group_check=True))
                    p.emit("tensor", lambda e, i=i: e.matmul(
                        self.ps_sc[:, 1024 + i * 128:1024 + (i + 1) * 128],
                        self.c_id16[:, :], self.c_od[:, ts(i)],
                        start=False, stop=True, skip_group_check=True))
                    dwl = [tr_ev[0]] if tr_ev[0] else []
                    p.emit("tensor", lambda e: e.transpose(
                        self.ps_p[0][:, 0:128], probs[:, 0:128],
                        self.c_id32[:, :]), waits=dwl, incs=[(pe, 1)])
                    w_sc = pe.count
                    p.emit("vector", lambda e, span=span: e.reduce_max(
                        s.stat[:, 0:1], self.ps_sc[:, 0:span], axis=AX),
                        waits=[(pe, w_sc)])
                    p.emit("vector", lambda e, span=span: e.reduce_max(
                        s.stat[:, 1:2], self.ps_sc[:, 1024:1024 + span],
                        axis=AX))
                    p.edge("vector")
                    p.emit("vector", lambda e: e.tensor_tensor(
                        s.stat[:, 0:1], s.stat[:, 0:1], s.stat[:, 1:2],
                        ALU.max))
                    p.edge("vector")
                    p.emit("vector", lambda e: e.tensor_scalar(
                        s.stat[:, 2:3], s.stat[:, 0:1], -SCALE, None, ALU.mult),
                        incs=[(dv, 1)])
                    w_max = dv.count
                    exp_wl = [(dv, w_max)]
                    p.emit("scalar", lambda e, span=span: e.activation(
                        probs[:, 0:span], self.ps_sc[:, 0:span], AF.Exp,
                        bias=s.stat[:, 2:3], scale=SCALE,
                        accum_out=s.stat[:, 4:5]), waits=exp_wl, incs=[(ac, 1)])
                    p.emit("scalar", lambda e, span=span: e.activation(
                        probs[:, span:2 * span],
                        self.ps_sc[:, 1024:1024 + span], AF.Exp,
                        bias=s.stat[:, 2:3], scale=SCALE,
                        accum_out=s.stat[:, 5:6]), incs=[(ac, 1)])
                    w_exp = ac.count
                    exp_done = (ac, w_exp)
                    p.emit("vector", lambda e: e.tensor_tensor(
                        s.stat[:, 4:5], s.stat[:, 4:5], s.stat[:, 5:6], ALU.add),
                        waits=[(ac, w_exp)])
                    p.edge("vector")
                    p.emit("vector", lambda e: e.reciprocal(
                        s.stat[:, 6:7], s.stat[:, 4:5]))
                    p.edge("vector")
                    p.emit("vector", lambda e, span=span: e.tensor_scalar(
                        probs[:, 0:2 * span], probs[:, 0:2 * span],
                        s.stat[:, 6:7], None, ALU.mult), incs=[(dv, 1)])
                    w_nrm = dv.count
                    nj = 2 * (i + 1)
                    w_trs = []
                    for j in range(nj):
                        bb = j % 2
                        wl = [(dv, w_nrm)]
                        if tr_ev[bb]:
                            wl.append(tr_ev[bb])
                        p.emit("tensor", lambda e, j=j, bb=bb: e.transpose(
                            self.ps_p[bb][:, 0:128], probs[:, ts(j)],
                            self.c_id32[:, :]), waits=wl, incs=[(pe, 1)])
                        w_trs.append(pe.count)
                        if j >= 1:
                            jp = j - 1
                            p.emit("scalar", lambda e, jp=jp: e.copy(
                                probsT[:, ts(jp)], self.ps_p[jp % 2][:, 0:128]),
                                waits=[(pe, w_trs[j])], incs=[(ac, 1)])
                            tr_ev[jp % 2] = (ac, ac.count)
                    ob = (nj) % 2
                    wl = [tr_ev[ob]] if tr_ev[ob] else []
                    p.emit("tensor", lambda e, ob=ob: e.transpose(
                        self.ps_p[ob][:, 0:128], probs[:, 0:128],
                        self.c_id32[:, :]), waits=wl, incs=[(pe, 1)])
                    w_dmy = pe.count
                    tr_ev[ob] = None
                    jp = nj - 1
                    p.emit("scalar", lambda e, jp=jp: e.copy(
                        probsT[:, ts(jp)], self.ps_p[jp % 2][:, 0:128]),
                        waits=[(pe, w_dmy)], incs=[(ac, 1)])
                    tr_ev[jp % 2] = (ac, ac.count)
                    tr_ev[ob] = (ac, ac.count)
                    w_trev = ac.count
                    for j in range(2 * (i + 1)):
                        vbase = j * 128 if j <= i else (j - (i + 1)) * 128 + span
                        p.emit("tensor", lambda e, j=j, vbase=vbase, i=i:
                               e.matmul(self.ps_sc[:, 0:128],
                                        v_sp[:, vbase:vbase + 128],
                                        probsT[:, ts(j)],
                                        start=(j == 0),
                                        stop=(j == 2 * i + 1)),
                               waits=([(ac, w_trev)] if j == 0 else []),
                               incs=([(pe, 1)] if j == 2 * i + 1 else []))
                    dmb = 0
                    wl = [tr_ev[dmb]] if tr_ev[dmb] else []
                    p.emit("tensor", lambda e, dmb=dmb: e.transpose(
                        self.ps_p[dmb][:, 0:128], probs[:, 0:128],
                        self.c_id32[:, :]), waits=wl, incs=[(pe, 1)])
                    tr_ev[dmb] = None
                    w_ctx = pe.count
                    ctx_wl = [(pe, w_ctx)]
                    if i == 0 and g == 0 and ctx_st_done:
                        ctx_wl.append(ctx_st_done)
                    if getattr(self, "_dump_hold", None):
                        ctx_wl.append(self._dump_hold)
                        self._dump_hold = None
                    p.emit("vector", lambda e, g=g, i=i: e.tensor_copy(
                        ctx_sb[:, g * 1024 + i * 128:g * 1024 + (i + 1) * 128],
                        self.ps_sc[:, 0:128]),
                        waits=ctx_wl, incs=[(dv, 1)])
                    ctx_ev = (dv, dv.count)
                    self.pull_deq(1)
                kv_free = (pe, pe.count)
            q_free = (pe, pe.count)
            cdst = self.ctx_dram[ts(kvh, G * DH), :].rearrange(
                "(g p) t -> p g t", p=128)
            wl = [ctx_ev] + ([ctx_st_done] if ctx_st_done else [])
            p.emit("scalar", lambda e, cdst=cdst: e.dma_start(
                out=cdst,
                in_=ctx_sb.ap().rearrange("p (g t) -> p g t", g=G)),
                waits=wl, incs=[(ld2, 16)])
            ctx_st_done = (ld2, ld2.count)

        self.free("q_cur", "k_sp", "v_sp", "probs", "probsT", "ctx_sb")
        self._attn_done = ctx_st_done

    def wo_phase(self):
        p, s = self.p, self
        ld, ld2, cast = self.s_ld, self.s_ld2, self.s_cast
        pe, dv, ac = self.s_pe, self.s_dv, self.s_ac

        ctx_f = self.sb("ctx_f", [128, H * NOWN * 128], BF16)
        wpan = [self.sb(f"wopan{i}", [128, DCH * 128], BF16) for i in range(2)]
        s_sb = [self.sb(f"s_sb{i}", [128, 512], F32) for i in range(2)]
        pan_free = [None, None]
        psum_free = [None, None]
        ssb_free = [None, None]

        csrc = self.ctx_dram.rearrange("(c p) t -> p c t", p=128)
        p.emit("sync", lambda e: e.dma_start(
            out=ctx_f.ap().rearrange("p (c t) -> p c t", c=H), in_=csrc),
               waits=[self._attn_done], incs=[(ld, 16)])
        w_ctx = ld.count

        for eo in range(DCH):
            src = self.woT[:, ts(eo)].rearrange("(c p) f -> p c f", p=128)
            p.emit("gpsimd", lambda e, o=wpan[eo % 2], i=src:
                   e.dma_start(
                       out=o.ap().rearrange("p (c f) -> p c f", f=128),
                       in_=i),
                   waits=([pan_free[eo % 2]] if pan_free[eo % 2] else []),
                   incs=[(cast, 16)])
            w_pan = cast.count
            for gg in range(2):
                psum = self.ps_p[gg]
                for hh in range(H):
                    wl = []
                    if hh == 0:
                        wl = [(cast, w_pan), (ld, w_ctx)]
                        if psum_free[gg]:
                            wl.append(psum_free[gg])
                        elif eo == 0:
                            wl.append((ac, ac.count))
                    p.emit("tensor", lambda e, eo=eo, gg=gg, hh=hh, psum=psum:
                           e.matmul(psum[:, :], wpan[eo % 2][:, ts(hh)],
                                    ctx_f[:, hh * 1024 + gg * 512:
                                          hh * 1024 + (gg + 1) * 512],
                                    start=(hh == 0), stop=(hh == H - 1)),
                           waits=wl, incs=([(pe, 1)] if hh == H - 1 else []))
                p.emit("tensor", lambda e: e.transpose(
                    self.ps_sc[:, 0:128], s_sb[0][:, 0:128], self.c_id32[:, :]),
                    waits=[(dv, dv.count)], incs=[(pe, 1)])
                w_acc = pe.count
                if gg == 1:
                    pan_free[eo % 2] = (pe, w_acc)
                sb_ = s_sb[gg]
                p.emit("vector", lambda e, sb_=sb_, psum=psum:
                       e.tensor_copy(sb_[:, :], psum[:, :]),
                       waits=[(pe, w_acc)] +
                             ([ssb_free[gg]] if ssb_free[gg] else []),
                       incs=[(dv, 1)])
                w_ev = dv.count
                psum_free[gg] = (dv, w_ev)
                p.emit("scalar", lambda e, eo=eo, gg=gg, sb_=sb_:
                       e.dma_start(out=self.s_dram[ts(eo), ts(gg, 512)],
                                   in_=sb_[:, :]),
                       waits=[(dv, w_ev)], incs=[(ld2, 16)])
                ssb_free[gg] = (ld2, ld2.count)
            self.pull_deq(1)
        self._s_done = (ld2, ld2.count)

        if self.stage == "attn":
            p.emit("sync", lambda e: e.dma_start(out=self.dbg_s, in_=self.s_dram),
                   waits=[self._s_done], incs=[(self.s_out, 16)])
            p.emit("sync", lambda e: e.dma_start(out=self.dbg_ctx,
                                                 in_=self.ctx_dram),
                   waits=[self._s_done], incs=[(self.s_out, 16)])
        self.free("ctx_f", "wopan0", "wopan1", "s_sb0", "s_sb1")

    def ffn(self):
        p, s = self.p, self
        ld, ld2 = self.s_ld, self.s_ld2
        pe, dv, ac, gp = self.s_pe, self.s_dv, self.s_ac, self.s_gp

        self.free("absT", "dq_in0", "dq_in1", "dq_red",
                  "dq_r0", "dq_as0",
                  "dq_B0", "dq_B1", "dq_A0", "dq_o0",
                  "stage8_0", "stage8_1")
        xn_g = self.sb("xn_g", [128, DCH * 512], BF16)
        h_fm = self.sb("h_fm", [128, HCH * 512], BF16)
        gpan = [self.sb(f"gpan{i}", [128, DCH * 128], BF16) for i in range(2)]
        upan = [self.sb(f"upan{i}", [128, DCH * 128], BF16) for i in range(2)]
        d3pan = [self.sb(f"d3pan{i}", [128, HCH * 128], BF16) for i in range(2)]
        sil = self.sb("sil", [128, 512], F32)
        s_tl = self.sb("s_tl", [128, 512], F32)
        x_tl = self.sb("x_tl", [128, 512], F32)
        o_tm_b = self.sb("o_tm", [128, 512], F32)
        sg = sil
        o_tm = [o_tm_b, o_tm_b]

        def agc(hc):
            col = hc * 128
            for k, (a, bb) in enumerate(W12_AG_CHUNKS):
                c0 = sum(W1_WINS[:a])
                c1 = sum(W1_WINS[:bb])
                if c0 <= col < c1:
                    return k, c0
            raise AssertionError

        gp_free = [None, None]
        up_free = [None, None]
        d3_free = [None, None]
        h_ev = None
        sil_free = None
        otm_free = [None, None]
        xn_ld_prev = None

        for tg in range(2):
            xsrc = self.xn_dram[:, ts(tg, 512)].rearrange(
                "(c p) t -> p c t", p=128)
            wl = [getattr(self, "_s_done")]
            if xn_ld_prev:
                wl.append(xn_ld_prev)
            if tg == 0:
                wl += [(dv, dv.count), (gp, gp.count), (ac, ac.count),
                       (pe, pe.count), (self.s_dqo, self.s_dqo.count),
                       (self.s_dqi, self.s_dqi.count)]
            p.emit("sync", lambda e, xsrc=xsrc: e.dma_start(
                out=xn_g.ap().rearrange("p (c t) -> p c t", c=DCH),
                in_=xsrc), waits=wl, incs=[(ld, 16)])
            w_xn = ld.count

            for hc in range(HCH):
                k, c0 = agc(hc)
                col = hc * 128 - c0
                wd = self.dq_full[("w1", k)].shape[1]
                g_src = self.dq_full[("w1", k)][:, col:col + 128].rearrange(
                    "(c p) f -> p c f", p=128)
                u_src = self.dq_full[("w2", k)][:, col:col + 128].rearrange(
                    "(c p) f -> p c f", p=128)
                wl = [(self.s_cc, self.ag_done[("w1", k)]),
                      (self.s_cc, self.ag_done[("w2", k)])]
                p.emit("sync", lambda e, o=gpan[hc % 2], i=g_src:
                       e.dma_start(
                           out=o.ap().rearrange("p (c f) -> p c f", f=128),
                           in_=i),
                       waits=wl + ([gp_free[hc % 2]] if gp_free[hc % 2] else []),
                       incs=[(ld, 16)])
                p.emit("sync", lambda e, o=upan[hc % 2], i=u_src:
                       e.dma_start(
                           out=o.ap().rearrange("p (c f) -> p c f", f=128),
                           in_=i),
                       waits=([up_free[hc % 2]] if up_free[hc % 2] else []),
                       incs=[(ld, 16)])
                w_pan = ld.count
                for dc in range(DCH):
                    wl = []
                    if dc == 0:
                        wl = [(ld, w_pan), (ld, w_xn)]
                        if h_ev:
                            wl.append(h_ev)
                        elif tg == 0 and hc == 0:
                            wl.append((self.s_ac, self.s_ac.count))
                    p.emit("tensor", lambda e, hc=hc, dc=dc:
                           e.matmul(self.ps_p[0][:, :], gpan[hc % 2][:, ts(dc)],
                                    xn_g[:, ts(dc, 512)],
                                    start=(dc == 0), stop=(dc == DCH - 1)),
                           waits=wl)
                    p.emit("tensor", lambda e, hc=hc, dc=dc:
                           e.matmul(self.ps_p[1][:, :], upan[hc % 2][:, ts(dc)],
                                    xn_g[:, ts(dc, 512)],
                                    start=(dc == 0), stop=(dc == DCH - 1)),
                           incs=([(pe, 1)] if dc == DCH - 1 else []))
                p.emit("tensor", lambda e: e.transpose(
                    self.ps_sc[:, 0:128], s_tl[:, 0:128], self.c_id32[:, :]),
                    waits=[(dv, dv.count)], incs=[(pe, 1)])
                w_acc = pe.count
                gp_free[hc % 2] = (pe, w_acc)
                up_free[hc % 2] = (pe, w_acc)
                p.emit("scalar", lambda e: e.activation(
                    sil[:, :], self.ps_p[0][:, :], AF.Silu),
                    waits=[(pe, w_acc)] + ([sil_free] if sil_free else []),
                    incs=[(ac, 1)])
                w_sil = ac.count
                hw = [(ac, w_sil)]
                if hc == 0:
                    hw.append((pe, pe.count))
                p.emit("vector", lambda e, hc=hc: e.tensor_tensor(
                    h_fm[:, ts(hc, 512)], sil[:, :], self.ps_p[1][:, :],
                    ALU.mult), waits=hw, incs=[(dv, 1)])
                h_ev = (dv, dv.count)
                sil_free = (dv, dv.count)
            w_h = dv.count

            for c in range(DCH):
                r = c // 4
                d3src = self.dq_full[("w3", 0)][
                    r * HID:(r + 1) * HID, (c % 4) * 128:(c % 4) * 128 + 128
                ].rearrange("(hc p) f -> p hc f", p=128)
                p.emit("sync", lambda e, o=d3pan[c % 2], i=d3src:
                       e.dma_start(
                           out=o.ap().rearrange("p (c f) -> p c f", f=128),
                           in_=i),
                       waits=[(self.s_cc, self.ag_done[("w3", 0)])] +
                             ([d3_free[c % 2]] if d3_free[c % 2] else []),
                       incs=[(ld, 16)])
                w_pan = ld.count
                psum = self.ps_p[c % 2]
                for hc in range(HCH):
                    wl = []
                    if hc == 0:
                        wl = [(ld, w_pan), (dv, w_h)]
                        pf = getattr(self, "_dn_free", [None, None])[c % 2]
                        if pf:
                            wl.append(pf)
                    p.emit("tensor", lambda e, c=c, hc=hc, psum=psum:
                           e.matmul(psum[:, :], d3pan[c % 2][:, ts(hc)],
                                    h_fm[:, ts(hc, 512)],
                                    start=(hc == 0), stop=(hc == HCH - 1)),
                           waits=wl, incs=([(pe, 1)] if hc == HCH - 1 else []))
                p.emit("tensor", lambda e, c=c: e.transpose(
                    self.ps_p[(c + 1) % 2][:, 0:128], s_tl[:, 0:128],
                    self.c_id32[:, :]),
                    waits=[(ac, ac.count)], incs=[(pe, 1)])
                w_oacc = pe.count
                d3_free[c % 2] = (pe, w_oacc)
                p.emit("scalar", lambda e, psum=psum: e.activation(
                    sil[:, :], psum[:, :], AF.Silu),
                    waits=[(pe, w_oacc)] + ([sil_free] if sil_free else []),
                    incs=[(ac, 1)])
                w_sil = ac.count
                p.emit("sync", lambda e, c=c, tg=tg: e.dma_start(
                    out=s_tl[:, :], in_=self.s_dram[ts(c), ts(tg, 512)]),
                    waits=[self._s_done, (dv, dv.count)], incs=[(ld, 16)])
                w_s = ld.count
                p.emit("vector", lambda e: e.tensor_tensor(
                    sg[:, :], sil[:, :], s_tl[:, :], ALU.mult),
                    waits=[(ac, w_sil), (ld, w_s), (pe, pe.count)],
                    incs=[(dv, 1)])
                w_sg = dv.count
                sil_free = (dv, w_sg)
                self._dn_free = getattr(self, "_dn_free", [None, None])
                self._dn_free[c % 2] = (dv, w_sg)
                for tl in range(4):
                    wl = [(dv, w_sg)]
                    asm_ev = getattr(self, "_asm_ev", None)
                    if c % 4 == 0 and tl == 0 and asm_ev:
                        wl.append(asm_ev)
                    if c % 4 == 0 and tl == 0:
                        wl.append((ac, ac.count))
                    p.emit("tensor", lambda e, tl=tl, c=c: e.transpose(
                        self.ps_sc[:, tl * 512 + (c % 4) * 128:
                                   tl * 512 + (c % 4) * 128 + 128],
                        sg[:, ts(tl)], self.c_id32[:, :]),
                        waits=wl, incs=([(pe, 1)] if tl == 3 else []))
                p.emit("tensor", lambda e, c=c: e.transpose(
                    self.ps_p[c % 2][:, 0:128], s_tl[:, 0:128],
                    self.c_id32[:, :]),
                    waits=[(ac, ac.count)], incs=[(pe, 1)])
                w_tr = pe.count
                sil_free = (pe, w_tr)
                if c % 4 == 3:
                    dr = c // 4
                    for tl in range(4):
                        tile = tg * 4 + tl
                        ob = o_tm[0]
                        p.emit("sync", lambda e, tile=tile, dr=dr: e.dma_start(
                            out=x_tl[:, :], in_=self.x[ts(tile), ts(dr, 512)]),
                            waits=[(dv, dv.count)], incs=[(ld, 16)])
                        w_x = ld.count
                        p.emit("vector", lambda e, tl=tl, ob=ob: e.tensor_tensor(
                            ob[:, :], self.ps_sc[:, ts(tl, 512)], x_tl[:, :],
                            ALU.add),
                            waits=[(pe, w_tr), (ld, w_x)] +
                                  ([otm_free[0]] if otm_free[0] else []),
                            incs=[(dv, 1)])
                        w_o = dv.count
                        if tl == 3:
                            self._asm_ev = (dv, w_o)
                        p.emit("scalar", lambda e, tile=tile, dr=dr, ob=ob:
                               e.dma_start(out=self.out[ts(tile), ts(dr, 512)],
                                           in_=ob[:, :]),
                               waits=[(dv, w_o)], incs=[(self.s_out, 16)])
                        otm_free[0] = (self.s_out, self.s_out.count)
            xn_ld_prev = (pe, pe.count)

    def finish(self):
        p = self.p
        p.emit("gpsimd", lambda e: e.nop(),
               waits=[(self.s_out, self.s_out.count),
                      (self.s_ld2, self.s_ld2.count),
                      (self.s_dqo, self.s_dqo.count),
                      (self.s_cc, self.s_cc.count),
                      (self.s_cast, self.s_cast.count)])

    def make(self):
        nc = self.build()
        with nc.Block() as block:
            self.p.install(block)
        for cm in reversed(list(self._cms.values())):
            cm.__exit__(None, None, None)
        self._cms.clear()
        return nc


_NC_CACHE = {}


def kernel(L0_attn_x, L0_attn_Wq, L0_attn_Wk, L0_attn_Wv, L0_attn_Wo,
           L0_ffn_w1, L0_ffn_w2, L0_ffn_w3):
    in_maps, perms = host_prep(L0_attn_x, L0_attn_Wq, L0_attn_Wk, L0_attn_Wv,
                               L0_attn_Wo, L0_ffn_w1, L0_ffn_w2, L0_ffn_w3)
    if "full" not in _NC_CACHE:
        _NC_CACHE["full"] = Builder("full").make()
    nc = _NC_CACHE["full"]
    res = run_bass_kernel_spmd(nc, in_maps, core_ids=list(range(8)))
    out = np.empty((B, T, E), dtype=np.float32)
    for c in range(8):
        rows = perms[c][:NOWN * 128]
        out[c // 2][rows] = res.results[c]["out"]
    return out
